# version 01
# baseline (speedup 1.0000x reference)
import numpy as np
import ml_dtypes
from contextlib import ExitStack

import concourse.bass as bass
import concourse.mybir as mybir
from concourse.bass_utils import run_bass_kernel_spmd

F32 = mybir.dt.float32
BF16 = mybir.dt.bfloat16
I32 = mybir.dt.int32
AF = mybir.ActivationFunctionType
ALU = mybir.AluOpType
AX = mybir.AxisListType

NCORES = 8
D = 4096
T = 1024
S = 8192
TB = 512
NB = T // TB
NC = D // 128
HID = 11008
NHT = HID // 128
EPS_LN = 1e-5
EPS_RMS = 1e-6
ISQ = float(1.0 / np.sqrt(128.0))
DBG_B = 0

ENGINES = ("sync", "tensor", "vector", "scalar", "gpsimd")


class Rot:

    def __init__(self, n):
        self.n = n
        self.slots = [None] * n

    def acquire(self, ctx, eng, i):
        m = self.slots[i % self.n]
        if m is None:
            return
        for mm in (m if isinstance(m, list) else [m]):
            ctx.waitm(eng, mm)

    def release(self, i, m):
        self.slots[i % self.n] = m


class Ctx:
    def __init__(self, nc, stack):
        self.nc = nc
        self.stack = stack
        self.ops = {e: [] for e in ENGINES}
        self.sems = {}
        self.cnt = {}
        self.rots = {}

    def sem(self, name):
        if name not in self.sems:
            self.sems[name] = self.stack.enter_context(self.nc.semaphore(name))
            self.cnt[name] = 0
        return self.sems[name]

    def rot(self, name, n):
        if name not in self.rots:
            self.rots[name] = Rot(n)
        return self.rots[name]

    def emit(self, eng, fn):
        self.ops[eng].append(fn)

    def wait(self, eng, sem_name, val):
        s = self.sem(sem_name)
        self.emit(eng, lambda e, s=s, v=val: e.wait_ge(s, v))

    def dma(self, eng, out, in_, stream):
        name = "D_" + stream
        s = self.sem(name)
        self.cnt[name] += 16
        v = self.cnt[name]
        self.emit(eng, lambda e, o=out, i=in_, s=s: e.dma_start(out=o, in_=i).then_inc(s, 16))
        return (name, v)

    def op(self, eng, fn, milestone=False):
        if not milestone:
            self.emit(eng, lambda e, f=fn: f(e))
            return None
        name = "C_" + eng
        s = self.sem(name)
        self.cnt[name] += 1
        v = self.cnt[name]
        self.emit(eng, lambda e, f=fn, s=s: f(e).then_inc(s, 1))
        return (name, v)

    def waitm(self, eng, m):
        if m is not None:
            self.wait(eng, m[0], m[1])

    def bacq(self, eng, i):
        self.rot("bk%d" % i, 1).acquire(self, eng, 0)

    def brel(self, i, m):
        self.rot("bk%d" % i, 1).release(0, m)


class Bufs(dict):
    def __init__(self, nc, stack):
        super().__init__()
        self.nc = nc
        self.stack = stack

    def sb(self, name, shape, dt):
        if name not in self:
            self[name] = self.stack.enter_context(self.nc.sbuf_tensor(name, shape, dt))
        return self[name]

    def bank(self, i):
        name = "psb%d" % i
        if name not in self:
            self[name] = self.stack.enter_context(self.nc.psum_tensor(name, [128, TB], F32))
        return self[name]



def blk_lhsT(w):
    din, dout = w.shape
    nci, noti = din // 128, dout // 128
    return np.ascontiguousarray(
        w.reshape(nci, 128, noti, 128).transpose(2, 1, 0, 3).reshape(noti, 128, nci * 128))


def host_prep(inputs):
    x = np.asarray(inputs["L0_attn_x"]).reshape(S, D)
    f32 = np.float32
    shared = {
        "c_identf": np.eye(128, dtype=f32),
        "c_identb": np.eye(128, dtype=ml_dtypes.bfloat16),
        "c_emat": np.kron(np.eye(8, dtype=f32), np.ones((1, 16), f32)),
        "c_ones_f": np.ones((1, TB), f32),
        "c_ones_b": np.ones((1, TB), ml_dtypes.bfloat16),
        "c_ones_colf": np.ones((128, 1), f32),
        "c_ones_colb": np.ones((128, 1), ml_dtypes.bfloat16),
        "wbT_blk": blk_lhsT(np.asarray(inputs["L0_ffn_W_base"]).T.astype(f32)),
        "b_base": np.asarray(inputs["L0_ffn_b_base"]).reshape(1, D).astype(f32),
        "acat_blk": np.ascontiguousarray(
            np.asarray(inputs["L0_ffn_lora_A"]).transpose(1, 0, 2)
            .reshape(NC, 128, 128).transpose(1, 0, 2)).astype(f32),
        "bcat": np.asarray(inputs["L0_ffn_lora_B"]).reshape(128, D).astype(f32),
        "wrT": np.ascontiguousarray(np.asarray(inputs["L0_ffn_W_router"]).T).astype(f32),
        "b_router": np.asarray(inputs["L0_ffn_b_router"]).reshape(1, 8).astype(f32),
        "wq0": np.asarray(inputs["L0_attn_wq"]).astype(f32),
        "wk0_blk": blk_lhsT(np.asarray(inputs["L0_attn_wk"]).astype(f32)),
        "wv0": np.asarray(inputs["L0_attn_wv"]).astype(f32),
        "wo0_blk": blk_lhsT(np.asarray(inputs["L0_attn_wo"]).astype(f32)),
    }
    shared.update({
        "Wq1": np.asarray(inputs["L1_attn_Wq"]).astype(f32),
        "Wk1": np.asarray(inputs["L1_attn_Wk"]).astype(f32),
        "Wv1": np.asarray(inputs["L1_attn_Wv"]).astype(f32),
        "wo1_blk": blk_lhsT(np.asarray(inputs["L1_attn_Wo"]).astype(f32)),
        "bq1": np.asarray(inputs["L1_attn_bq"]).reshape(1, D).astype(f32),
        "bk1": np.asarray(inputs["L1_attn_bk"]).reshape(1, D).astype(f32),
        "bv1": np.asarray(inputs["L1_attn_bv"]).reshape(1, D).astype(f32),
        "bo1": np.asarray(inputs["L1_attn_bo"]).reshape(1, D).astype(f32),
        "gq1": np.asarray(inputs["L1_attn_gq"]).reshape(1, 128).astype(f32),
        "gk1": np.asarray(inputs["L1_attn_gk"]).reshape(1, 128).astype(f32),
    })
    w1_4d = np.asarray(inputs["L1_ffn_w1"]).reshape(HID, 32, 64)
    w3_4d = np.asarray(inputs["L1_ffn_w3"]).reshape(HID, 32, 64)
    w2_4d = np.asarray(inputs["L1_ffn_w2"]).reshape(D, 86, 64)
    s1_2d = np.asarray(inputs["L1_ffn_s1"]).reshape(HID, 64)
    s3_2d = np.asarray(inputs["L1_ffn_s3"]).reshape(HID, 64)
    s2_2d = np.asarray(inputs["L1_ffn_s2"]).reshape(D, 172)
    per_core = []
    for r in range(NCORES):
        xs = np.ascontiguousarray(x[r * T:(r + 1) * T, :].T).astype(f32)
        rows = slice(r * 11 * 128, min((r * 11 + 11) * 128, HID))
        n = rows.stop - rows.start
        w1s = np.zeros((11 * 128, 32, 64), np.int32)
        w3s = np.zeros((11 * 128, 32, 64), np.int32)
        s1s = np.zeros((11 * 128, 64), f32)
        s3s = np.zeros((11 * 128, 64), f32)
        w1s[:n] = w1_4d[rows]
        w3s[:n] = w3_4d[rows]
        s1s[:n] = s1_2d[rows]
        s3s[:n] = s3_2d[rows]
        tsl = slice(r * 11, min(r * 11 + 11, 86))
        nt = tsl.stop - tsl.start
        w2s = np.zeros((D, 11, 64), np.int32)
        s2s = np.zeros((D, 22), f32)
        w2s[:, :nt] = w2_4d[:, tsl]
        s2s[:, :2 * nt] = s2_2d[:, 2 * tsl.start:2 * tsl.stop]
        per_core.append({
            "x_t": xs,
            "w1s": np.ascontiguousarray(w1s.reshape(11 * 128, 2048)),
            "w3s": np.ascontiguousarray(w3s.reshape(11 * 128, 2048)),
            "w2s": np.ascontiguousarray(w2s.reshape(D, 704)),
            "s1s": s1s, "s3s": s3s, "s2s": s2s,
            **shared})
    return per_core



def emit_ln(ctx, B, X_src, b, xln, tag):
    xc = B.sb("ln_xc", [128, 2 * TB], F32)
    sq = B.sb("ln_sq", [128, 2 * TB], F32)
    rows = B.sb("ln_rows", [1, 8 * TB], F32)
    ab = B.sb("ln_ab", [128, 2 * TB], F32)
    tmp = B.sb("ln_tmp", [128, 2 * TB], F32)
    ones_row = B.sb("ones_row_f", [1, TB], F32)
    ones_col = B.sb("ones_col_f", [128, 1], F32)
    ps_sum = B.bank(0)[0:1, :]
    ps_sq = B.bank(1)[0:1, :]
    ps_repA = B.bank(2)[:, :]
    ps_repB = B.bank(3)[:, :]

    r_xc = ctx.rot("ln_xc", 2)
    r_sq = ctx.rot("ln_sq", 2)
    r_tmp = ctx.rot("ln_tmp", 2)

    vp = None
    for c in range(NC):
        sl = slice((c % 2) * TB, (c % 2 + 1) * TB)
        r_xc.acquire(ctx, "sync", c)
        lv = ctx.dma("sync", xc[:, sl],
                     X_src[c * 128:(c + 1) * 128, b * TB:(b + 1) * TB], "lnx%d" % (c % 2))
        ctx.waitm("tensor", lv)
        if c == 0:
            ctx.bacq("tensor", 0)
            ctx.bacq("tensor", 1)
        ctx.op("tensor", lambda e, sl=sl, st=(c == 0), sp=(c == NC - 1): e.matmul(
            ps_sum, ones_col[:, :], xc[:, sl], start=st, stop=sp))
        ctx.waitm("vector", lv)
        r_sq.acquire(ctx, "vector", c)
        va = ctx.op("vector", lambda e, sl=sl: e.tensor_mul(sq[:, sl], xc[:, sl], xc[:, sl]),
                    milestone=True)
        ctx.waitm("tensor", va)
        vp = ctx.op("tensor", lambda e, sl=sl, st=(c == 0), sp=(c == NC - 1): e.matmul(
            ps_sq, ones_col[:, :], sq[:, sl], start=st, stop=sp), milestone=True)
        r_sq.release(c, vp)
        r_xc.release(c, vp)
    stats = vp

    R_mean = rows[0:1, 0:TB]
    R_msq = rows[0:1, TB:2 * TB]
    R_var = rows[0:1, 2 * TB:3 * TB]
    R_ve = rows[0:1, 5 * TB:6 * TB]
    R_ln = rows[0:1, 6 * TB:7 * TB]
    R_rstd = rows[0:1, 3 * TB:4 * TB]
    R_b = rows[0:1, 4 * TB:5 * TB]
    ctx.waitm("vector", stats)
    ctx.rot("ln_rows_g", 1).acquire(ctx, "vector", 0)
    ctx.op("vector", lambda e: e.tensor_scalar_mul(R_mean, ps_sum, 1.0 / D))
    v = ctx.op("vector", lambda e: e.tensor_scalar_mul(R_msq, ps_sq, 1.0 / D), True)
    ctx.brel(0, v)
    ctx.brel(1, v)
    ctx.waitm("vector", v)
    v = ctx.op("vector", lambda e: e.scalar_tensor_tensor(
        R_var, R_mean, -1.0, R_mean, ALU.mult, ALU.mult), True)
    ctx.waitm("vector", v)
    v = ctx.op("vector", lambda e: e.scalar_tensor_tensor(
        R_ve, R_var, EPS_LN, R_msq, ALU.add, ALU.add), True)
    ctx.waitm("scalar", v)
    a = ctx.op("scalar", lambda e: e.activation(R_ln, R_ve, AF.Ln), True)
    ctx.waitm("scalar", a)
    a = ctx.op("scalar", lambda e: e.activation(R_rstd, R_ln, AF.Exp, scale=-0.5), True)
    ctx.waitm("vector", a)
    v = ctx.op("vector", lambda e: e.scalar_tensor_tensor(
        R_b, R_mean, -1.0, R_rstd, ALU.mult, ALU.mult), True)
    ctx.waitm("tensor", a)
    ctx.bacq("tensor", 2)
    ctx.op("tensor", lambda e: e.matmul(ps_repA, ones_row[0:1, 0:128], R_rstd,
                                        start=True, stop=True))
    ctx.waitm("tensor", v)
    ctx.bacq("tensor", 3)
    vr = ctx.op("tensor", lambda e: e.matmul(ps_repB, ones_row[0:1, 0:128], R_b,
                                             start=True, stop=True), True)
    ctx.waitm("vector", vr)
    ctx.rot("ln_rows_g", 1).release(0, vr)
    ctx.rot("ln_ab_g", 1).acquire(ctx, "vector", 0)
    ctx.op("vector", lambda e: e.tensor_copy(ab[:, 0:TB], ps_repA))
    v = ctx.op("vector", lambda e: e.tensor_copy(ab[:, TB:2 * TB], ps_repB), True)
    ctx.brel(2, v)
    ctx.brel(3, v)
    ctx.waitm("vector", v)

    last = None
    ctx.rot("xln_guard", 1).acquire(ctx, "vector", 0)
    for c in range(NC):
        sl = slice((c % 2) * TB, (c % 2 + 1) * TB)
        r_xc.acquire(ctx, "sync", NC + c)
        lv = ctx.dma("sync", xc[:, sl],
                     X_src[c * 128:(c + 1) * 128, b * TB:(b + 1) * TB], "lnx%d" % (c % 2))
        ctx.waitm("vector", lv)
        r_tmp.acquire(ctx, "vector", c)
        vm = ctx.op("vector", lambda e, sl=sl: e.tensor_mul(tmp[:, sl], xc[:, sl], ab[:, 0:TB]),
                    True)
        r_xc.release(NC + c, vm)
        ctx.waitm("vector", vm)
        last = ctx.op("vector", lambda e, sl=sl, c=c: e.tensor_add(
            xln[:, c * TB:(c + 1) * TB], tmp[:, sl], ab[:, TB:2 * TB]), True)
        r_tmp.release(c, last)
    return last


def emit_delta_store(ctx, B, X_src, X_dst, b, ps_tile, ot, tag, bias_col=None,
                     scale=None):
    xr = B.sb(f"dres_{tag}", [128, 2 * TB], F32)
    osb = B.sb(f"dout_{tag}", [128, 2 * TB], F32)
    r_x = ctx.rot(f"dres_{tag}", 2)
    r_o = ctx.rot(f"dout_{tag}", 2)
    sl = slice((ot % 2) * TB, (ot % 2 + 1) * TB)
    r_x.acquire(ctx, "sync", ot)
    lv = ctx.dma("sync", xr[:, sl],
                 X_src[ot * 128:(ot + 1) * 128, b * TB:(b + 1) * TB], f"{tag}xr{ot % 2}")
    ctx.waitm("vector", lv)
    r_o.acquire(ctx, "vector", ot)
    if bias_col is not None:
        dv = ctx.op("vector", lambda e: e.scalar_tensor_tensor(
            osb[:, sl], ps_tile, bias_col, xr[:, sl], ALU.add, ALU.add), True)
    elif scale is not None:
        dv = ctx.op("vector", lambda e: e.scalar_tensor_tensor(
            osb[:, sl], ps_tile, scale, xr[:, sl], ALU.mult, ALU.add), True)
    else:
        dv = ctx.op("vector", lambda e: e.tensor_add(osb[:, sl], ps_tile, xr[:, sl]), True)
    r_x.release(ot, dv)
    ctx.waitm("gpsimd", dv)
    sv = ctx.dma("gpsimd", X_dst[ot * 128:(ot + 1) * 128, b * TB:(b + 1) * TB], osb[:, sl],
                 f"{tag}os{ot % 2}")
    r_o.release(ot, sv)
    return dv, sv



def emit_ffn0_setup(ctx, B, io):
    wr_sb = B.sb("f0_wr", [128, NC * 8], BF16)
    br_sb = B.sb("f0_br", [1, 8], F32)
    bb_sb = B.sb("f0_bb", [1, D], F32)
    acat_sb = B.sb("f0_acat", [128, NC * 128], BF16)
    bcat_f = B.sb("f0_bcat_f", [128, D], F32)
    bcat_sb = B.sb("f0_bcat", [128, D], BF16)
    br_rep = B.sb("f0_brrep", [128, 8], F32)
    bbT = B.sb("f0_bbT", [128, NC], F32)
    ones_row = B.sb("ones_row_f", [1, TB], F32)
    identf = B.sb("identf", [128, 128], F32)
    ps_scr = B.bank(6)

    l1 = ctx.dma("gpsimd", wr_sb[:].rearrange("p (c e) -> p c e", e=8),
                 io["wrT"].rearrange("(c p) e -> p c e", p=128), "su_wr")
    l2 = ctx.dma("gpsimd", acat_sb[:].rearrange("p (c i) -> p c i", i=128),
                 io["acat_blk"], "su_acat")
    l3 = ctx.dma("sync", bcat_f[:], io["bcat"], "su_bcat")
    ctx.waitm("vector", l3)
    c4 = ctx.op("vector", lambda e: e.tensor_scalar(bcat_sb[:], bcat_f[:], 0.5, None,
                                                    ALU.mult), True)
    l4 = ctx.dma("sync", br_sb[:], io["b_router"], "su_br")
    row = B.sb("f0_rowscr", [1, D], F32)
    l5 = ctx.dma("sync", row[:], io["b_base"], "su_bb")
    ctx.waitm("vector", l5)
    c2 = ctx.op("vector", lambda e: e.tensor_scalar(bb_sb[:], row[:], 0.5, None, ALU.mult),
                True)
    ctx.waitm("tensor", l4)
    ctx.bacq("tensor", 6)
    pv = ctx.op("tensor", lambda e: e.matmul(ps_scr[:, 0:8], ones_row[0:1, 0:128],
                                             br_sb[0:1, :], start=True, stop=True), True)
    ctx.waitm("vector", pv)
    c0 = ctx.op("vector", lambda e: e.tensor_copy(br_rep[:], ps_scr[:, 0:8]), True)
    ctx.waitm("tensor", c2)
    pv = None
    for ot in range(NC):
        pv = ctx.op("tensor", lambda e, ot=ot: e.transpose(
            ps_scr[:, 8 + ot: 9 + ot], bb_sb[0:1, ot * 128:(ot + 1) * 128],
            identf[0:1, 0:1]), milestone=(ot == NC - 1))
    ctx.waitm("vector", pv)
    c3 = ctx.op("vector", lambda e: e.tensor_copy(bbT[:], ps_scr[:, 8:8 + NC]), True)
    ctx.brel(6, c3)
    ctx.waitm("tensor", c3)
    ctx.waitm("tensor", c0)
    ctx.waitm("tensor", c4)
    ctx.waitm("gpsimd", l1)
    ctx.waitm("gpsimd", l2)
    ctx.wait("tensor", "D_su_wr", 16)
    ctx.wait("tensor", "D_su_acat", 16)


def emit_sparsemax(ctx, B, z_src_ps):
    br_rep = B.sb("f0_brrep", [128, 8], F32)
    zbuf = B.sb("f0_z", [128, 8], F32)
    spx = B.sb("f0_spx", [128, 200], F32)
    dm = spx[:, 0:64].rearrange("p (i j) -> p i j", i=8)
    ge = spx[:, 64:128].rearrange("p (i j) -> p i j", i=8)
    gz = spx[:, 128:192].rearrange("p (i j) -> p i j", i=8)
    cnt = spx[:, 192:200]

    z = zbuf[:]
    v = ctx.op("vector", lambda e: e.tensor_add(z, z_src_ps, br_rep[:]), True)
    ctx.waitm("vector", v)
    v = ctx.op("vector", lambda e: e.tensor_sub(
        dm, z.unsqueeze(2).broadcast_to([128, 8, 8]),
        z.unsqueeze(1).broadcast_to([128, 8, 8])), True)
    ctx.waitm("vector", v)
    v = ctx.op("vector", lambda e: e.tensor_single_scalar(ge, dm, 0.0, ALU.is_le), True)
    ctx.waitm("vector", v)
    ctx.op("vector", lambda e: e.reduce_sum(cnt.unsqueeze(2), ge, axis=AX.X))
    v = ctx.op("vector", lambda e: e.tensor_mul(
        gz, ge, z.unsqueeze(1).broadcast_to([128, 8, 8])), True)
    ctx.waitm("vector", v)
    Si = spx[:, 0:8]
    v = ctx.op("vector", lambda e: e.reduce_sum(Si.unsqueeze(2), gz, axis=AX.X), True)
    u = spx[:, 8:16]
    ctx.waitm("vector", v)
    v = ctx.op("vector", lambda e: e.tensor_mul(u, cnt, z), True)
    ctx.waitm("vector", v)
    u2 = spx[:, 16:24]
    v = ctx.op("vector", lambda e: e.tensor_sub(u2, u, Si), True)
    ctx.waitm("vector", v)
    sup = spx[:, 24:32]
    v = ctx.op("vector", lambda e: e.tensor_single_scalar(sup, u2, -1.0, ALU.is_gt), True)
    ctx.waitm("vector", v)
    kz = spx[:, 32:33]
    ctx.op("vector", lambda e: e.reduce_sum(kz, sup, axis=AX.X))
    tz = spx[:, 34:42]
    v = ctx.op("vector", lambda e: e.tensor_mul(tz, sup, z), True)
    ctx.waitm("vector", v)
    ts_ = spx[:, 33:34]
    v = ctx.op("vector", lambda e: e.reduce_sum(ts_, tz, axis=AX.X), True)
    ctx.waitm("vector", v)
    rk = spx[:, 42:43]
    ctx.op("vector", lambda e: e.reciprocal(rk, kz))
    tsm1 = spx[:, 43:44]
    v = ctx.op("vector", lambda e: e.tensor_scalar_add(tsm1, ts_, -1.0), True)
    ctx.waitm("vector", v)
    tau = spx[:, 44:45]
    v = ctx.op("vector", lambda e: e.tensor_mul(tau, tsm1, rk), True)
    ctx.waitm("vector", v)
    w = spx[:, 48:56]
    v = ctx.op("vector", lambda e: e.tensor_scalar(w, z, tau, 0.0, ALU.subtract, ALU.max),
               True)
    return w, v


def emit_ffn0(ctx, B, io):
    X_src, X_dst = io["X_src"], io["X_dst"]
    dbg = io.get("dbg")
    xln = B.sb("xln", [128, NC * TB], BF16)
    identf = B.sb("identf", [128, 128], F32)
    emat = B.sb("emat", [8, 128], F32)
    wr_sb = B.sb("f0_wr", [128, NC * 8], BF16)
    acat_sb = B.sb("f0_acat", [128, NC * 128], BF16)
    bcat_sb = B.sb("f0_bcat", [128, D], BF16)
    bbT = B.sb("f0_bbT", [128, NC], F32)

    ps_r = B.bank(0)[:, 0:32]
    ps_h = B.bank(1)[:, :]
    ps_wt = B.bank(2)[0:8, :]
    ps_wr = B.bank(3)[:, :]
    ps_o = [B.bank(4)[:, :], B.bank(5)[:, :]]

    wt_b = B.sb("f0_wtb", [128, 2 * D], BF16)
    wt_h = B.sb("f0_wth", [128, 2 * D], BF16)
    wts = B.sb("f0_wts", [8, TB], F32)
    wrep = B.sb("f0_wrep", [128, TB], F32)
    gts = B.sb("f0_g", [128, TB], BF16)

    r_wt = ctx.rot("f0_wt", 2)
    r_wth = ctx.rot("f0_wth", 2)

    for b in range(NB):
        ln_done = emit_ln(ctx, B, X_src, b, xln, "f0")
        if dbg and b == DBG_B:
            ctx.waitm("gpsimd", ln_done)
            dv = ctx.dma("gpsimd", dbg["d_xln"], xln[:], "dbg")
            ctx.waitm("gpsimd", dv)

        ctx.waitm("tensor", ln_done)
        ctx.bacq("tensor", 0)
        rt_done = None
        for qt in range(4):
            for c in range(NC):
                rt_done = ctx.op("tensor", lambda e, qt=qt, c=c: e.matmul(
                    ps_r[:, qt * 8:(qt + 1) * 8],
                    xln[:, c * TB + qt * 128: c * TB + (qt + 1) * 128],
                    wr_sb[:, c * 8:(c + 1) * 8], start=(c == 0), stop=(c == NC - 1)),
                    milestone=(qt == 3 and c == NC - 1))
        ctx.bacq("tensor", 1)
        h_done = None
        for c in range(NC):
            h_done = ctx.op("tensor", lambda e, c=c: e.matmul(
                ps_h, acat_sb[:, c * 128:(c + 1) * 128],
                xln[:, c * TB:(c + 1) * TB], start=(c == 0), stop=(c == NC - 1)),
                milestone=(c == NC - 1))

        ctx.waitm("vector", rt_done)
        tdone = None
        r_spx = ctx.rot("f0_spx", 1)
        for qt in range(4):
            r_spx.acquire(ctx, "vector", 0)
            w, wv = emit_sparsemax(ctx, B, ps_r[:, qt * 8:(qt + 1) * 8])
            if dbg and b == DBG_B:
                ctx.waitm("gpsimd", wv)
                dv = ctx.dma("gpsimd", dbg["d_w"][:, qt * 8:(qt + 1) * 8], w, "dbg")
                ctx.waitm("vector", dv)
            ctx.waitm("tensor", wv)
            if qt == 0:
                ctx.bacq("tensor", 2)
            tdone = ctx.op("tensor", lambda e, w=w, qt=qt: e.transpose(
                ps_wt[:, qt * 128:(qt + 1) * 128], w, identf[:]), milestone=True)
            r_spx.release(0, tdone)
        ctx.waitm("vector", tdone)
        v = ctx.op("vector", lambda e: e.tensor_copy(wts[:], ps_wt), True)
        ctx.brel(2, v)
        ctx.brel(0, tdone)
        ctx.waitm("tensor", v)
        ctx.bacq("tensor", 3)
        v = ctx.op("tensor", lambda e: e.matmul(ps_wr, emat[:], wts[:],
                                                start=True, stop=True), True)
        ctx.waitm("vector", v)
        v = ctx.op("vector", lambda e: e.tensor_copy(wrep[:], ps_wr), True)
        ctx.brel(3, v)
        ctx.waitm("vector", v)
        ctx.waitm("vector", h_done)
        g_done = ctx.op("vector", lambda e: e.tensor_mul(gts[:], ps_h, wrep[:]), True)
        ctx.brel(1, g_done)

        for ot in range(NC):
            par = ot % 2
            wsl = slice(par * D, (par + 1) * D)
            r_wt.acquire(ctx, "gpsimd", ot)
            lv = ctx.dma("gpsimd", wt_b[:, wsl], io["wbT_blk"][ot], "f0w%d" % par)
            ctx.waitm("vector", lv)
            r_wth.acquire(ctx, "vector", ot)
            cv = ctx.op("vector", lambda e, wsl=wsl: e.tensor_scalar(
                wt_h[:, wsl], wt_b[:, wsl], 0.5, None, ALU.mult), True)
            r_wt.release(ot, cv)
            psl = ps_o[par]
            ctx.waitm("tensor", cv)
            ctx.bacq("tensor", 4 + par)
            for c in range(NC):
                ctx.op("tensor", lambda e, psl=psl, wsl=wsl, c=c: e.matmul(
                    psl, wt_h[:, wsl.start + c * 128: wsl.start + (c + 1) * 128],
                    xln[:, c * TB:(c + 1) * TB], start=(c == 0), stop=False))
            if ot == 0:
                ctx.waitm("tensor", g_done)
            vo = ctx.op("tensor", lambda e, psl=psl, ot=ot: e.matmul(
                psl, bcat_sb[:, ot * 128:(ot + 1) * 128], gts[:],
                start=False, stop=True), True)
            r_wth.release(ot, vo)
            ctx.waitm("vector", vo)
            dv, sv = emit_delta_store(ctx, B, X_src, X_dst, b, psl, ot, "f0",
                                      bias_col=bbT[:, ot:ot + 1])
            ctx.brel(4 + par, dv)
            if ot == NC - 1:
                ctx.rot("xln_guard", 1).release(0, vo)





def emit_attn_core(ctx, B, tag, nheads, q_dram, kview_fn, vview_fn, ao, qsb, pbuf,
                   arA, arB):
    nc = ctx.nc
    ones_row = B.sb("ones_row_f", [1, TB], F32)
    ones_row_b = B.sb("ones_row_b", [1, TB], BF16)
    ones_col_b = B.sb("ones_col_b", [128, 1], BF16)
    identf = B.sb("identf", [128, 128], F32)
    def getbuf(name, shape, dt):
        if name in B:
            v = B[name]
            return v if isinstance(v, bass.AP) else v[:]
        return B.sb(name, shape, dt)[:]
    mxbuf = getbuf(tag + "_mx", [128, 64], F32)
    negm = getbuf(tag + "_negm", [128, 4], F32)
    mrow_f = getbuf(tag + "_mrowf", [1, TB], F32)
    mrow_b = getbuf(tag + "_mrowb", [1, TB], BF16)
    rsum = getbuf(tag + "_rsum", [1, TB], F32)
    rrep = getbuf(tag + "_rrep", [128, TB], F32)

    r_kq = ctx.rot(tag + "_kq", 2)
    r_mx = ctx.rot(tag + "_mxr", 1)
    r_pb = ctx.rot(tag + "_pb", 4)
    r_mr = ctx.rot(tag + "_mr", 1)
    r_rr = ctx.rot(tag + "_rr", 1)
    r_ao = ctx.rot(tag + "_ao", 1)

    psA = [B.bank(0), B.bank(1)]
    psT = [B.bank(2), B.bank(3)]
    psO = B.bank(4)
    psSum = B.bank(5)[0:1, :]
    psR = B.bank(6)
    psM = B.bank(7)[0:1, :]

    end_pe = None
    for h in range(nheads):
        par = h % 2
        ksb = arA[:, par * 8192:(par + 1) * 8192]
        vsb = arB[:, par * 8192:(par + 1) * 8192]
        qsl = qsb[:, par * T:(par + 1) * T]
        r_kq.acquire(ctx, "sync", h)
        lk = ctx.dma("sync", ksb.rearrange("p (r t) -> p r t", r=NCORES), kview_fn(h),
                     tag + "kh%d" % par)
        nt8 = T // 128
        for r in range(NCORES):
            lv_ = ctx.dma("sync",
                          vsb[:, r * nt8 * 128:(r + 1) * nt8 * 128]
                          .rearrange("p (t8 d) -> p t8 d", d=128),
                          vview_fn(h, r), tag + "vh%d" % par)
        lq = ctx.dma("sync", qsl, q_dram[h * 128:(h + 1) * 128, :], tag + "qh%d" % par)
        ctx.waitm("tensor", lk)
        ctx.waitm("tensor", lv_)
        ctx.waitm("tensor", lq)
        for b in range(NB):
            r_mx.acquire(ctx, "vector", 0)
            dvem = None
            finm = None
            for qt in range(4):
                for kc in range(16):
                    i = qt * 16 + kc
                    ctx.bacq("tensor", i % 2)
                    ma = ctx.op("tensor", lambda e, qt=qt, kc=kc, i=i, qsl=qsl,
                                ksb=ksb, b=b: e.matmul(
                        psA[i % 2][:, :],
                        qsl[:, b * TB + qt * 128: b * TB + (qt + 1) * 128],
                        ksb[:, kc * TB:(kc + 1) * TB], start=True, stop=True), True)
                    ctx.waitm("vector", ma)
                    dvem = ctx.op("vector", lambda e, qt=qt, kc=kc, i=i: e.reduce_max(
                        mxbuf[:, i:i + 1], psA[i % 2][:, :], axis=AX.X), True)
                    ctx.brel(i % 2, dvem)
                ctx.waitm("vector", dvem)
                finm = ctx.op("vector", lambda e, qt=qt: e.reduce_max(
                    negm[:, qt:qt + 1], mxbuf[:, qt * 16:(qt + 1) * 16], axis=AX.X,
                    negate=True), True)
            ctx.waitm("tensor", finm)
            r_mr.acquire(ctx, "tensor", 0)
            ctx.bacq("tensor", 7)
            trm = None
            for qt in range(4):
                trm = ctx.op("tensor", lambda e, qt=qt: e.transpose(
                    psM[0:1, qt * 128:(qt + 1) * 128], negm[:, qt:qt + 1], identf[:]),
                    milestone=(qt == 3))
            ctx.waitm("vector", trm)
            cmv = ctx.op("vector", lambda e: e.tensor_copy(mrow_f, psM), True)
            ctx.brel(7, cmv)
            ctx.waitm("gpsimd", cmv)
            mbv = ctx.dma("gpsimd", mrow_b, mrow_f, tag + "mb")
            ctx.waitm("tensor", mbv)
            mpv = None
            msum = None
            for kt in range(64):
                tpar = kt % 2
                ctx.bacq("tensor", 2 + tpar)
                ctx.op("tensor", lambda e, kt=kt, tpar=tpar, ksb=ksb, qsl=qsl,
                       b=b: e.matmul(
                    psT[tpar][:, :], ksb[:, kt * 128:(kt + 1) * 128],
                    qsl[:, b * TB:(b + 1) * TB], start=True, stop=False))
                mt = ctx.op("tensor", lambda e, tpar=tpar: e.matmul(
                    psT[tpar][:, :], ones_row_b[0:1, 0:128], mrow_b[0:1, :],
                    start=False, stop=True), True)
                ctx.waitm("scalar", mt)
                r_pb.acquire(ctx, "scalar", kt)
                aex = ctx.op("scalar", lambda e, kt=kt, tpar=tpar: e.activation(
                    pbuf[:, (kt % 4) * TB:(kt % 4 + 1) * TB], psT[tpar][:, :], AF.Exp),
                    True)
                ctx.brel(2 + tpar, aex)
                ctx.waitm("tensor", aex)
                if kt == 0:
                    ctx.bacq("tensor", 4)
                    ctx.bacq("tensor", 5)
                mpv = ctx.op("tensor", lambda e, kt=kt, vsb=vsb: e.matmul(
                    psO[:, :], vsb[:, kt * 128:(kt + 1) * 128],
                    pbuf[:, (kt % 4) * TB:(kt % 4 + 1) * TB],
                    start=(kt == 0), stop=(kt == 63)), milestone=(kt == 63))
                msum = ctx.op("tensor", lambda e, kt=kt: e.matmul(
                    psSum, ones_col_b[:, :], pbuf[:, (kt % 4) * TB:(kt % 4 + 1) * TB],
                    start=(kt == 0), stop=(kt == 63)), True)
                r_pb.release(kt, msum)
            ctx.waitm("vector", msum)
            rv = ctx.op("vector", lambda e: e.reciprocal(rsum, psSum), True)
            ctx.brel(5, rv)
            ctx.waitm("tensor", rv)
            ctx.bacq("tensor", 6)
            r_rr.acquire(ctx, "tensor", 0)
            mr = ctx.op("tensor", lambda e: e.matmul(psR[:, :], ones_row[0:1, 0:128],
                                                     rsum, start=True, stop=True), True)
            ctx.waitm("vector", mr)
            rrv = ctx.op("vector", lambda e: e.tensor_copy(rrep, psR[:, :]), True)
            ctx.brel(6, rrv)
            ctx.waitm("vector", rrv)
            ctx.waitm("vector", mpv)
            slot = h * NB + b
            r_ao.acquire(ctx, "vector", 0)
            mao = ctx.op("vector", lambda e, slot=slot: e.tensor_mul(
                ao[:, slot * TB:(slot + 1) * TB], psO[:, :], rrep[:]), True)
            ctx.brel(4, mao)
            r_rr.release(0, mao)
            r_mr.release(0, mao)
            r_mx.release(0, mao)
            if b == NB - 1:
                end_pe = msum
                r_kq.release(h, msum)
    return end_pe


def emit_attn0(ctx, B, io):
    X_src, X_dst = io["X_src"], io["X_dst"]
    xln = B.sb("xln", [128, NC * TB], BF16)
    arA = B.sb("ar_a", [128, 8192], F32)
    arB = B.sb("ar_b", [128, 8192], F32)
    arAb = arA[:].bitcast(BF16)
    arBb = arB[:].bitcast(BF16)
    q0t = io["q0t"]
    kv0c = io["kv0c"]
    kv0g = io["kv0g"]

    wqf = arAb
    eff1 = B.sb("a0_eff1", [128, 1024], F32)
    eff2 = B.sb("a0_eff2", [128, 1024], F32)
    eff3 = B.sb("a0_eff3", [128, 1024], F32)
    effb = B.sb("a0_effb", [128, 2 * 1024], BF16)
    evac = B.sb("a0_evac", [128, 2 * TB], BF16)
    r_wq = ctx.rot("a0_wq", 2)
    r_eff = ctx.rot("a0_eff", 2)
    r_evac = ctx.rot("a0_evac", 2)
    r_wk = ctx.rot("a0_wk", 2)
    wkb = B.sb("a0_wkb", [128, 2 * 4096], BF16)

    for b in range(NB):
        ln_done = emit_ln(ctx, B, X_src, b, xln, "a0")
        ctx.waitm("tensor", ln_done)

        ebm = None
        for c in range(NC):
            par = c % 2
            r_wq.acquire(ctx, "sync", c)
            lw = ctx.dma("sync", arA[:, par * 4096:(par + 1) * 4096],
                         io["wq0"][c * 128:(c + 1) * 128, :], "a0wq%d" % par)
            ctx.waitm("vector", lw)
            r_eff.acquire(ctx, "vector", c)
            v4 = arA[:, par * 4096:(par + 1) * 4096].rearrange(
                "p (h g d) -> p h g d", g=4, d=128)
            e1 = ctx.op("vector", lambda e, v4=v4: e.tensor_add(
                eff1[:].rearrange("p (h d) -> p h d", d=128), v4[:, :, 0, :],
                v4[:, :, 1, :]), True)
            e2 = ctx.op("vector", lambda e, v4=v4: e.tensor_add(
                eff2[:].rearrange("p (h d) -> p h d", d=128), v4[:, :, 2, :],
                v4[:, :, 3, :]), True)
            ctx.waitm("vector", e2)
            e3 = ctx.op("vector", lambda e: e.tensor_add(eff3[:], eff1[:], eff2[:]), True)
            r_wq.release(c, e3)
            ctx.waitm("vector", e3)
            ebm = ctx.op("vector", lambda e, par=par: e.tensor_scalar(
                effb[:, par * 1024:(par + 1) * 1024], eff3[:], ISQ, None, ALU.mult), True)
            ctx.waitm("tensor", ebm)
            if c == 0:
                for i in range(8):
                    ctx.bacq("tensor", i)
            qm = None
            for ot in range(8):
                qm = ctx.op("tensor", lambda e, ot=ot, par=par, c=c: e.matmul(
                    B.bank(ot)[:, :], effb[:, par * 1024 + ot * 128: par * 1024 + (ot + 1) * 128],
                    xln[:, c * TB:(c + 1) * TB], start=(c == 0), stop=(c == NC - 1)),
                    milestone=(ot == 7))
            r_eff.release(c, qm)
        ctx.waitm("vector", qm)
        for ot in range(8):
            par = ot % 2
            r_evac.acquire(ctx, "vector", ot)
            ev = ctx.op("vector", lambda e, ot=ot, par=par: e.tensor_copy(
                evac[:, par * TB:(par + 1) * TB], B.bank(ot)[:, :]), True)
            ctx.brel(ot, ev)
            ctx.waitm("gpsimd", ev)
            sv = ctx.dma("gpsimd", q0t[ot * 128:(ot + 1) * 128, b * TB:(b + 1) * TB],
                         evac[:, par * TB:(par + 1) * TB], "a0qs%d" % par)
            r_evac.release(ot, sv)

        for ot in range(8):
            par = ot % 2
            r_wk.acquire(ctx, "gpsimd", ot)
            lw = ctx.dma("gpsimd", wkb[:, par * 4096:(par + 1) * 4096],
                         io["wk0_blk"][ot], "a0wk%d" % par)
            ctx.waitm("tensor", lw)
            ctx.bacq("tensor", par)
            mk = None
            for c in range(NC):
                mk = ctx.op("tensor", lambda e, ot=ot, par=par, c=c: e.matmul(
                    B.bank(par)[:, :], wkb[:, par * 4096 + c * 128: par * 4096 + (c + 1) * 128],
                    xln[:, c * TB:(c + 1) * TB], start=(c == 0), stop=(c == NC - 1)),
                    milestone=(c == NC - 1))
            r_wk.release(ot, mk)
            ctx.waitm("vector", mk)
            r_evac.acquire(ctx, "vector", ot)
            ev = ctx.op("vector", lambda e, par=par: e.tensor_copy(
                evac[:, par * TB:(par + 1) * TB], B.bank(par)[:, :]), True)
            ctx.brel(par, ev)
            ctx.waitm("gpsimd", ev)
            sv = ctx.dma("gpsimd", kv0c[ot * 128:(ot + 1) * 128, b * TB:(b + 1) * TB],
                         evac[:, par * TB:(par + 1) * TB], "a0kc%d" % par)
            r_evac.release(ot, sv)

        r_wv = ctx.rot("a0_wv", 1)
        for vs in range(2):
            r_wv.acquire(ctx, "gpsimd", 0)
            lw = ctx.dma("gpsimd",
                         arBb[:, 0:NC * TB].rearrange("p (c v) -> p c v", v=TB),
                         io["wv0"].rearrange("(c p) v -> p c v", p=128)[:, :, vs * TB:(vs + 1) * TB],
                         "a0wv")
            ctx.waitm("tensor", lw)
            vm = None
            for tt in range(4):
                par = tt % 2
                ctx.bacq("tensor", par)
                for c in range(NC):
                    vm = ctx.op("tensor", lambda e, tt=tt, par=par, c=c, vs=vs: e.matmul(
                        B.bank(par)[:, :],
                        xln[:, c * TB + tt * 128: c * TB + (tt + 1) * 128],
                        arBb[:, c * TB:(c + 1) * TB], start=(c == 0), stop=(c == NC - 1)),
                        milestone=(c == NC - 1))
                ctx.waitm("vector", vm)
                r_evac.acquire(ctx, "vector", tt)
                ev = ctx.op("vector", lambda e, par=par: e.tensor_copy(
                    evac[:, par * TB:(par + 1) * TB], B.bank(par)[:, :]), True)
                ctx.brel(par, ev)
                ctx.waitm("gpsimd", ev)
                sv = ctx.dma("gpsimd",
                             kv0c[T + b * TB + tt * 128: T + b * TB + (tt + 1) * 128,
                                  vs * TB:(vs + 1) * TB],
                             evac[:, par * TB:(par + 1) * TB], "a0vc%d" % par)
                r_evac.release(tt, sv)
            r_wv.release(0, vm)

    for stream in ("a0kc0", "a0kc1", "a0vc0", "a0vc1", "a0qs0", "a0qs1"):
        nm = "D_" + stream
        if nm in ctx.cnt:
            ctx.wait("gpsimd", nm, ctx.cnt[nm])
    ccs = ctx.sem("cc_a0")
    ctx.cnt["cc_a0"] += 1
    ctx.emit("gpsimd", lambda e, s=ccs, i=kv0c, o=kv0g: e.collective_compute(
        "AllGather", ALU.bypass, replica_groups=[list(range(NCORES))],
        ins=[i.opt()], outs=[o.ap().opt()]).then_inc(s))
    ctx.wait("sync", "cc_a0", 1)
    ctx.wait("tensor", "cc_a0", 1)

    ao = xln[:, 0:16 * TB]
    qsb = xln[:, 24 * TB:28 * TB]
    pbuf = xln[:, 28 * TB:32 * TB]
    gv = kv0g.rearrange("(r w) t -> w r t", w=2 * T)

    def kview_fn(h):
        return gv[h * 128:(h + 1) * 128, :, 0:T]

    def vview_fn(h, r):
        return gv[T:2 * T, r, h * 128:(h + 1) * 128].rearrange(
            "(t8 p) d -> p t8 d", p=128)

    end_pe = emit_attn_core(ctx, B, "a0", 8, q0t, kview_fn, vview_fn, ao, qsb, pbuf,
                            arAb, arBb)

    wosb = B.sb("a0_wosb", [128, 2 * 1024], BF16)
    r_wo = ctx.rot("a0_wo", 2)
    for b in range(NB):
        for ot in range(NC):
            par = ot % 2
            r_wo.acquire(ctx, "gpsimd", b * NC + ot)
            lw = ctx.dma("gpsimd", wosb[:, par * 1024:(par + 1) * 1024],
                         io["wo0_blk"][ot], "a0wo%d" % par)
            ctx.waitm("tensor", lw)
            ctx.bacq("tensor", par)
            pm = None
            for h in range(8):
                slot = h * NB + b
                pm = ctx.op("tensor", lambda e, par=par, h=h, slot=slot: e.matmul(
                    B.bank(par)[:, :], wosb[:, par * 1024 + h * 128: par * 1024 + (h + 1) * 128],
                    ao[:, slot * TB:(slot + 1) * TB], start=(h == 0), stop=(h == 7)),
                    milestone=(h == 7))
            r_wo.release(b * NC + ot, pm)
            ctx.waitm("vector", pm)
            dv, sv = emit_delta_store(ctx, B, X_src, X_dst, b, B.bank(par)[:, :], ot, "a0")
            ctx.brel(par, dv)
            if b == NB - 1 and ot == NC - 1:
                ctx.rot("xln_guard", 1).release(0, pm)





def emit_dequant(ctx, B, io):
    mcs = io["mcs"]
    gms = io["gms"]
    identb = B.sb("identb", [128, 128], BF16)

    hts = B.sb("f1_hts", [128, NHT * TB], BF16)
    hts_i = hts[:].bitcast(I32)
    hts_f = hts[:].bitcast(F32)
    pk = hts_i[:, 0:4096]
    tmp1 = hts_i[:, 4096:6144]
    tmp2 = hts_i[:, 6144:8192]
    dqo = hts[:, 16384:20480]
    stg = hts[:, 20480:24576]
    s1sb = hts_f[:, 12288:12992]
    s3sb = hts_f[:, 12992:13696]
    s2sb = hts_f[:, 13696:14400]

    _kinds = globals().get("_DQ_KINDS", [0, 1, 2])
    if 0 in _kinds:
        l1 = ctx.dma("sync", s1sb.rearrange("p (rc g) -> p rc g", g=64),
                     io["s1s"].rearrange("(rc p) g -> p rc g", p=128), "dqsc1")
        ctx.waitm("vector", l1)
    if 1 in _kinds:
        l2 = ctx.dma("sync", s3sb.rearrange("p (rc g) -> p rc g", g=64),
                     io["s3s"].rearrange("(rc p) g -> p rc g", p=128), "dqsc3")
        ctx.waitm("vector", l2)
    if 2 in _kinds:
        l3 = ctx.dma("sync", s2sb.rearrange("p (rc g) -> p rc g", g=22),
                     io["s2s"].rearrange("(rc p) g -> p rc g", p=128), "dqsc2")
        ctx.waitm("vector", l3)

    r_pk = ctx.rot("dq_pk", 2)
    r_dqo = ctx.rot("dq_dqo", 1)
    r_stg = ctx.rot("dq_stg", 1)
    r_tp = ctx.rot("dq_tp", 2)
    psb_t = [B.bank(6), B.bank(7)]

    def do_block(kind, idx):
        par = (idx + kind * 11) % 2
        pks = pk[:, par * 2048:(par + 1) * 2048]
        r_pk.acquire(ctx, "sync", idx + kind * 11)
        if kind < 2:
            w = io["w1s"] if kind == 0 else io["w3s"]
            lv = ctx.dma("sync", pks,
                         w[idx * 128:(idx + 1) * 128, :], "dqp%d" % par)
            ssb = s1sb if kind == 0 else s3sb
            sv = ssb.rearrange("p (rc g) -> p rc g", g=64)[:, idx, :]
            nt = 32
        else:
            lv = ctx.dma("sync",
                         pks[0:128, 0:32 * 64].rearrange("p (rc j) -> p rc j", j=64),
                         io["w2s"].rearrange("(rc p) (t j) -> p rc t j", p=128, j=64)
                         [:, :, idx, :], "dqp%d" % par)
            nt = 32
        ctx.waitm("vector", lv)
        skip = globals().get("_DQ_SKIP", set())
        r_dqo.acquire(ctx, "vector", 0)
        pv3 = pks.rearrange("p (t j) -> p t j", j=64)
        if "C" in skip:
            v1 = ctx.op("vector", lambda e: e.memset(dqo, 0.0), True)
            ctx.waitm("tensor", v1)
            r_pk.release(idx + kind * 11, v1)
        else:
            v1 = ctx.op("vector", lambda e, pv3=pv3: e.tensor_single_scalar(
            tmp1.rearrange("p (t j) -> p t j", j=64), pv3, 4,
            ALU.arith_shift_right), True)
            v2 = ctx.op("vector", lambda e, pv3=pv3: e.tensor_scalar(
            tmp2.rearrange("p (t j) -> p t j", j=64), pv3, 15, 8,
            ALU.bitwise_and, ALU.bitwise_xor), True)
            dview = dqo.rearrange("p (t h j) -> p t h j", h=2, j=64)
            if kind < 2:
                sm = sv.rearrange("p (t h) -> p t h", h=2)
                smsb = sm[:, :, 0].unsqueeze(2).broadcast_to([128, 32, 64])
                slsb = sm[:, :, 1].unsqueeze(2).broadcast_to([128, 32, 64])
            else:
                s2v = s2sb.rearrange("p (rc g) -> p rc g", g=22)
                smsb = s2v[:, :, 2 * idx].unsqueeze(2).broadcast_to([128, 32, 64])
                slsb = s2v[:, :, 2 * idx + 1].unsqueeze(2).broadcast_to([128, 32, 64])
            ctx.waitm("vector", v1)
            v3 = ctx.op("vector", lambda e, smsb=smsb, dview=dview: e.tensor_mul(
                dview[:, :, 0, :], tmp1.rearrange("p (t j) -> p t j", j=64), smsb), True)
            ctx.waitm("vector", v2)
            v4 = ctx.op("vector", lambda e, slsb=slsb, dview=dview: e.scalar_tensor_tensor(
                dview[:, :, 1, :], tmp2.rearrange("p (t j) -> p t j", j=64), -8.0, slsb,
                ALU.add, ALU.mult), True)
            r_pk.release(idx + kind * 11, v4)
            ctx.waitm("tensor", v3)
            ctx.waitm("tensor", v4)
        r_stg.acquire(ctx, "vector", 0)
        last_ev = None
        if "T" in skip:
            last_ev = ctx.op("vector", lambda e: e.memset(stg, 0.0), True)
            mtp = last_ev
        for t in (range(nt) if "T" not in skip else []):
            tp = t % 2
            pslice = psb_t[tp][:, 0:64].bitcast(BF16)
            r_tp.acquire(ctx, "tensor", t)
            ctx.bacq("tensor", 6 + tp)
            mtp = ctx.op("tensor", lambda e, t=t, pslice=pslice: e.transpose(
                pslice, dqo[:, t * 128:(t + 1) * 128], identb[:]), True)
            ctx.waitm("vector", mtp)
            ev = ctx.op("vector", lambda e, t=t, pslice=pslice: e.tensor_copy(
                stg[:, t * 128:(t + 1) * 128], pslice), True)
            r_tp.release(t, ev)
            ctx.brel(6 + tp, ev)
            last_ev = ev
        r_dqo.release(0, mtp)
        ctx.waitm("gpsimd", last_ev)
        row0 = idx * 128
        sv2 = ctx.dma("gpsimd", mcs[kind][row0:row0 + 128, :], stg[:], "dqst")
        r_stg.release(0, sv2)
        return sv2

    last = None
    for kind in _kinds:
        for idx in range(11):
            last = do_block(kind, idx)
    ctx.waitm("gpsimd", last)
    ctx.wait("gpsimd", "D_dqst", ctx.cnt["D_dqst"])
    for kind in _kinds:
        ccs = ctx.sem("cc_f1%d" % kind)
        ctx.cnt["cc_f1%d" % kind] += 1
        ctx.emit("gpsimd", lambda e, s=ccs, i=mcs[kind], o=gms[kind]: e.collective_compute(
            "AllGather", ALU.bypass, replica_groups=[list(range(NCORES))],
            ins=[i.opt()], outs=[o.ap().opt()]).then_inc(s))
        ctx.wait("sync", "cc_f1%d" % kind, 1)
        ctx.wait("tensor", "cc_f1%d" % kind, 1)
        ctx.wait("vector", "cc_f1%d" % kind, 1)
    ctx.rot("f1_hts", 1).release(0, ("cc_f1%d" % _kinds[-1], 1))


def emit_ffn1(ctx, B, io):
    X_src, X_dst = io["X_src"], io["X_dst"]
    g1 = io["gms"][0].ap()
    g3 = io["gms"][1].ap()
    g2 = io["gms"][2].ap()
    xln = B.sb("xln", [128, NC * TB], BF16)
    hts = B.sb("f1_hts", [128, NHT * TB], BF16)
    w13 = B.sb("f1_w13", [128, 4 * 4096], BF16)
    m2col = [w13[:, 0:88 * 128], xln[:, 0:88 * 128]]
    silu = B.sb("f1_silu", [128, TB], F32)

    r_w1 = ctx.rot("f1_w1", 2)
    r_w3 = ctx.rot("f1_w3", 2)
    r_m2 = ctx.rot("f1_m2", 2)
    r_hts = ctx.rot("f1_hts", 1)
    r_silu = ctx.rot("f1_silu", 1)

    for b in range(NB):
        ln_done = emit_ln(ctx, B, X_src, b, xln, "f1")
        ctx.waitm("tensor", ln_done)
        r_hts.acquire(ctx, "vector", 0)
        hm = None
        for ht in range(NHT):
            par = ht % 2
            row0 = ht * 128
            r_w1.acquire(ctx, "sync", ht)
            lw1 = ctx.dma("sync", w13[:, par * 4096:(par + 1) * 4096],
                          g1[row0:row0 + 128, :], "f1w1%d" % par)
            r_w3.acquire(ctx, "sync", ht)
            lw3 = ctx.dma("sync", w13[:, (2 + par) * 4096:(3 + par) * 4096],
                          g3[row0:row0 + 128, :], "f1w3%d" % par)
            ctx.waitm("tensor", lw1)
            ctx.waitm("tensor", lw3)
            ctx.bacq("tensor", 2 * par)
            ctx.bacq("tensor", 2 * par + 1)
            ma = mb = None
            for c in range(NC):
                ma = ctx.op("tensor", lambda e, c=c, par=par: e.matmul(
                    B.bank(2 * par)[:, :],
                    w13[:, par * 4096 + c * 128: par * 4096 + (c + 1) * 128],
                    xln[:, c * TB:(c + 1) * TB], start=(c == 0), stop=(c == NC - 1)),
                    milestone=(c == NC - 1))
            for c in range(NC):
                mb = ctx.op("tensor", lambda e, c=c, par=par: e.matmul(
                    B.bank(2 * par + 1)[:, :],
                    w13[:, (2 + par) * 4096 + c * 128: (2 + par) * 4096 + (c + 1) * 128],
                    xln[:, c * TB:(c + 1) * TB], start=(c == 0), stop=(c == NC - 1)),
                    milestone=(c == NC - 1))
            r_w1.release(ht, ma)
            r_w3.release(ht, mb)
            ctx.waitm("scalar", ma)
            r_silu.acquire(ctx, "scalar", 0)
            sa = ctx.op("scalar", lambda e, par=par: e.activation(
                silu[:], B.bank(2 * par)[:, :], AF.Silu), True)
            ctx.brel(2 * par, sa)
            ctx.waitm("vector", sa)
            ctx.waitm("vector", mb)
            hm = ctx.op("vector", lambda e, ht=ht, par=par: e.tensor_mul(
                hts[:, ht * TB:(ht + 1) * TB], silu[:], B.bank(2 * par + 1)[:, :]), True)
            r_silu.release(0, hm)
            ctx.brel(2 * par + 1, hm)
        ctx.waitm("tensor", hm)
        r_m2.release(0, mb)
        r_m2.release(1, mb)
        pm_last = None
        for ot in range(NC):
            par = ot % 2
            r_m2.acquire(ctx, "sync", ot)
            lm = None
            for rank in range(NCORES):
                srcv = g2[rank * 1408:(rank + 1) * 1408, ot * 128:(ot + 1) * 128] \
                    .rearrange("(tl p) i -> p tl i", p=128)
                lm = ctx.dma("sync",
                             m2col[par][:, rank * 11 * 128:(rank + 1) * 11 * 128]
                             .rearrange("p (tl i) -> p tl i", i=128),
                             srcv, "f1m2%d" % par)
            ctx.waitm("tensor", lm)
            ctx.bacq("tensor", 4 + par)
            pm = None
            for hc in range(NHT):
                pm = ctx.op("tensor", lambda e, hc=hc, par=par: e.matmul(
                    B.bank(4 + par)[:, :],
                    m2col[par][:, hc * 128:(hc + 1) * 128],
                    hts[:, hc * TB:(hc + 1) * TB], start=(hc == 0), stop=(hc == NHT - 1)),
                    milestone=(hc == NHT - 1))
            r_m2.release(ot, pm)
            ctx.waitm("vector", pm)
            dv, sv = emit_delta_store(ctx, B, X_src, X_dst, b, B.bank(4 + par)[:, :],
                                      ot, "f1", scale=0.5)
            ctx.brel(4 + par, dv)
            pm_last = pm
        r_hts.release(0, pm_last)
        ctx.rot("xln_guard", 1).release(0, pm_last)





def emit_attn1(ctx, B, io):
    X_src, X_dst = io["X_src"], io["X_dst"]
    xln = B.sb("xln", [128, NC * TB], BF16)
    arA = B.sb("ar_a", [128, 8192], F32)
    arB = B.sb("ar_b", [128, 8192], F32)
    arAb = arA[:].bitcast(BF16)
    arBb = arB[:].bitcast(BF16)
    ao = B.sb("a1_ao", [128, 64 * TB], BF16)
    brep = ao[:].bitcast(F32)[:, 0:4096]
    sscr = ao[:].bitcast(F32)[:, 4096:4096 + 640]
    identf = B.sb("identf", [128, 128], F32)
    identb = B.sb("identb", [128, 128], BF16)
    ones_row = B.sb("ones_row_f", [1, TB], F32)
    grep = B.sb("a1_grep", [128, 2 * 128], F32)
    boT = B.sb("a1_boT", [128, NC], F32)
    rowb = B.sb("ln_rows", [1, 8 * TB], F32)[0:1, 0:D]
    r_rows = ctx.rot("ln_rows_g", 1)
    B["a1_evq"] = B.sb("ln_ab", [128, 2 * TB], F32)
    evq = B["a1_evq"]
    evb = B.sb("ln_tmp", [128, 2 * TB], F32)[:].bitcast(BF16)[:, 0:2 * TB]
    q1t = io["q1t"]
    k1c = io["k1c"]
    v1c = io["v1c"]
    g_k1 = io["g_k1"]
    g_v1 = io["g_v1"]

    r_wsl = ctx.rot("a1_wsl", 2)
    r_evq = ctx.rot("a1_evq", 2)
    r_evb = ctx.rot("a1_evb", 2)
    r_brep = ctx.rot("a1_brep", 1)
    r_grep = ctx.rot("a1_grep", 1)

    r_rows.acquire(ctx, "sync", 0)
    lg = ctx.dma("sync", rowb[0:1, 0:128], io["gq1"], "a1gq")
    ctx.waitm("tensor", lg)
    ctx.bacq("tensor", 6)
    pv = ctx.op("tensor", lambda e: e.matmul(B.bank(6)[:, 0:128], ones_row[0:1, 0:128],
                                             rowb[0:1, 0:128], start=True, stop=True), True)
    ctx.waitm("vector", pv)
    cg = ctx.op("vector", lambda e: e.tensor_scalar(
        grep[:, 0:128], B.bank(6)[:, 0:128], ISQ, None, ALU.mult), True)
    ctx.waitm("sync", cg)
    lg = ctx.dma("sync", rowb[0:1, 0:128], io["gk1"], "a1gq")
    ctx.waitm("tensor", lg)
    pv = ctx.op("tensor", lambda e: e.matmul(B.bank(6)[:, 128:256], ones_row[0:1, 0:128],
                                             rowb[0:1, 0:128], start=True, stop=True), True)
    ctx.waitm("vector", pv)
    cg2 = ctx.op("vector", lambda e: e.tensor_copy(grep[:, 128:256], B.bank(6)[:, 128:256]),
                 True)
    ctx.waitm("sync", cg2)
    lb = ctx.dma("sync", rowb[:], io["bo1"], "a1bo")
    ctx.waitm("tensor", lb)
    tv = None
    for ot in range(NC):
        tv = ctx.op("tensor", lambda e, ot=ot: e.transpose(
            B.bank(6)[:, 256 + ot:257 + ot], rowb[0:1, ot * 128:(ot + 1) * 128],
            identf[0:1, 0:1]), milestone=(ot == NC - 1))
    ctx.waitm("vector", tv)
    cb = ctx.op("vector", lambda e: e.tensor_copy(boT[:], B.bank(6)[:, 256:256 + NC]), True)
    ctx.brel(6, cb)
    r_rows.release(0, tv)
    ctx.waitm("tensor", cb)

    def build_brep(bname, stream):
        r_brep.acquire(ctx, "sync", 0)
        r_rows.acquire(ctx, "sync", 0)
        lv = ctx.dma("sync", rowb[:], io[bname], stream)
        ctx.waitm("tensor", lv)
        ctx.bacq("tensor", 7)
        mv = None
        for s8 in range(8):
            mv = ctx.op("tensor", lambda e, s8=s8: e.matmul(
                B.bank(7)[:, :], ones_row[0:1, 0:128], rowb[0:1, s8 * TB:(s8 + 1) * TB],
                start=True, stop=True), milestone=(s8 == 7)) if False else \
                ctx.op("tensor", lambda e, s8=s8: e.matmul(
                    B.bank(7)[:, :], ones_row[0:1, 0:128],
                    rowb[0:1, s8 * TB:(s8 + 1) * TB], start=True, stop=True), True)
            ctx.waitm("vector", mv)
            cv = ctx.op("vector", lambda e, s8=s8: e.tensor_copy(
                brep[:, s8 * TB:(s8 + 1) * TB], B.bank(7)[:, :]), True)
            ctx.brel(7, cv)
            ctx.bacq("tensor", 7)
        ctx.brel(7, cv)
        r_rows.release(0, mv)
        return cv

    for b in range(NB):
        ln_done = emit_ln(ctx, B, X_src, b, xln, "a1")
        ctx.waitm("tensor", ln_done)
        for mat, wname, bname, has_norm, gsl, dst in (
                (0, "Wq1", "bq1", True, slice(0, 128), "q"),
                (1, "Wk1", "bk1", True, slice(128, 256), "k"),
                (2, "Wv1", "bv1", False, None, "v")):
            bdone = build_brep(bname, "a1b%d" % mat)
            for hs in range(8):
                par = hs % 2
                wsl = arAb if par == 0 else arBb
                r_wsl.acquire(ctx, "gpsimd", hs)
                lw = ctx.dma("gpsimd",
                             wsl[:, 0:NC * TB].rearrange("p (c v) -> p c v", v=TB),
                             io[wname].rearrange("(c p) v -> p c v", p=128)
                             [:, :, hs * TB:(hs + 1) * TB], "a1w%d" % par)
                ctx.waitm("tensor", lw)
                mlast = None
                for tt in range(4):
                    tpar = tt % 2
                    ctx.bacq("tensor", tpar)
                    mm = None
                    for c in range(NC):
                        mm = ctx.op("tensor", lambda e, c=c, tt=tt, tpar=tpar, wsl=wsl:
                                    e.matmul(
                            B.bank(tpar)[:, :],
                            xln[:, c * TB + tt * 128: c * TB + (tt + 1) * 128],
                            wsl[:, c * TB:(c + 1) * TB],
                            start=(c == 0), stop=(c == NC - 1)), milestone=(c == NC - 1))
                    mlast = mm
                    ctx.waitm("vector", mm)
                    ctx.waitm("vector", bdone)
                    r_evq.acquire(ctx, "vector", tt)
                    ev = ctx.op("vector", lambda e, tpar=tpar, hs=hs: e.tensor_add(
                        evq[:, tpar * TB:(tpar + 1) * TB], B.bank(tpar)[:, :],
                        brep[:, hs * TB:(hs + 1) * TB]), True)
                    ctx.brel(tpar, ev)
                    r_evb.acquire(ctx, "vector", tpar)
                    evl = None
                    for h4 in range(4):
                        qsl_ = evq[:, tpar * TB + h4 * 128: tpar * TB + (h4 + 1) * 128]
                        osl = evb[:, tpar * TB + h4 * 128: tpar * TB + (h4 + 1) * 128]
                        if has_norm:
                            ctx.waitm("vector", ev if h4 == 0 else evl)
                            v1 = ctx.op("vector", lambda e, qsl_=qsl_: e.tensor_mul(
                                sscr[:, 0:128], qsl_, qsl_), True)
                            ctx.waitm("vector", v1)
                            v2 = ctx.op("vector", lambda e: e.reduce_sum(
                                sscr[:, 128:129], sscr[:, 0:128], axis=AX.X), True)
                            ctx.waitm("vector", v2)
                            v3 = ctx.op("vector", lambda e: e.tensor_scalar(
                                sscr[:, 129:130], sscr[:, 128:129], 1.0 / 128.0,
                                EPS_RMS, ALU.mult, ALU.add), True)
                            ctx.waitm("scalar", v3)
                            a1_ = ctx.op("scalar", lambda e: e.activation(
                                sscr[:, 130:131], sscr[:, 129:130], AF.Ln), True)
                            ctx.waitm("scalar", a1_)
                            a2_ = ctx.op("scalar", lambda e: e.activation(
                                sscr[:, 131:132], sscr[:, 130:131], AF.Exp, scale=-0.5),
                                True)
                            ctx.waitm("vector", a2_)
                            v4 = ctx.op("vector", lambda e, qsl_=qsl_: e.tensor_scalar(
                                sscr[:, 256:384], qsl_, sscr[:, 131:132], None,
                                ALU.mult), True)
                            ctx.waitm("vector", v4)
                            evl = ctx.op("vector", lambda e, osl=osl, gsl=gsl:
                                         e.tensor_mul(osl, sscr[:, 256:384],
                                                      grep[:, gsl]), True)
                        else:
                            ctx.waitm("vector", ev if h4 == 0 else evl)
                            evl = ctx.op("vector", lambda e, qsl_=qsl_, osl=osl:
                                         e.tensor_copy(osl, qsl_), True)
                    r_evq.release(tt, evl)
                    if dst == "v":
                        ctx.waitm("gpsimd", evl)
                        sv = ctx.dma("gpsimd",
                                     v1c[b * TB + tt * 128: b * TB + (tt + 1) * 128,
                                         hs * TB:(hs + 1) * TB],
                                     evb[:, tpar * TB:(tpar + 1) * TB], "a1vc%d" % tpar)
                        r_evb.release(tpar, sv)
                    else:
                        ctx.waitm("tensor", evl)
                        for h4 in range(4):
                            tb_ = h4 % 2
                            ctx.bacq("tensor", 4 + tb_)
                            ttv = ctx.op("tensor", lambda e, tpar=tpar, h4=h4, tb_=tb_:
                                         e.transpose(
                                B.bank(4 + tb_)[:, 0:64].bitcast(BF16),
                                evb[:, tpar * TB + h4 * 128: tpar * TB + (h4 + 1) * 128],
                                identb[:]), True)
                            ctx.waitm("vector", ttv)
                            cv2 = ctx.op("vector", lambda e, tb_=tb_, h4=h4: e.tensor_copy(
                                evb[:, (2 + 0) * 0 + h4 * 0:0] if False else
                                sscr[:, 384 + tb_ * 64:384 + tb_ * 64 + 64].bitcast(BF16),
                                B.bank(4 + tb_)[:, 0:64].bitcast(BF16)), True)
                            ctx.brel(4 + tb_, cv2)
                            ctx.waitm("gpsimd", cv2)
                            row0 = hs * TB + h4 * 128
                            dstt = q1t if dst == "q" else k1c
                            sv = ctx.dma("gpsimd",
                                         dstt[row0:row0 + 128,
                                              b * TB + tt * 128: b * TB + (tt + 1) * 128],
                                         sscr[:, 384 + tb_ * 64:384 + tb_ * 64 + 64]
                                         .bitcast(BF16), "a1ts%d" % tb_)
                            ctx.rot("a1_tsc", 2).release(tb_, sv)
                            ctx.rot("a1_tsc", 2).acquire(ctx, "vector", tb_ + 1)
                            if h4 == 3:
                                r_evb.release(tpar, ttv)
                r_wsl.release(hs, mlast)
        ctx.rot("ln_ab_g", 1).release(0, evl)
        ctx.rot("ln_tmp", 2).release(0, [s for s in r_evb.slots if s is not None])
        ctx.rot("ln_tmp", 2).release(1, [s for s in r_evb.slots if s is not None])

    for stream in ("a1ts0", "a1ts1", "a1vc0", "a1vc1"):
        nm = "D_" + stream
        if nm in ctx.cnt:
            ctx.wait("gpsimd", nm, ctx.cnt[nm])
    for nmc, i, o in (("cc_a1k", k1c, g_k1), ("cc_a1v", v1c, g_v1)):
        ccs = ctx.sem(nmc)
        ctx.cnt[nmc] += 1
        ctx.emit("gpsimd", lambda e, s=ccs, i=i, o=o: e.collective_compute(
            "AllGather", ALU.bypass, replica_groups=[list(range(NCORES))],
            ins=[i.opt()], outs=[o.ap().opt()]).then_inc(s))
        ctx.wait("sync", nmc, 1)
        ctx.wait("tensor", nmc, 1)
        ctx.wait("vector", nmc, 1)
        ctx.wait("scalar", nmc, 1)
        ctx.wait("gpsimd", nmc, 1)

    B["a1_mx"] = grep[:, 0:64]
    B["a1_negm"] = grep[:, 64:68]
    B["a1_rrep"] = evq[:, 0:TB]
    B["a1_mrowf"] = evq[0:1, TB:2 * TB]
    evb_f = evb[:].bitcast(F32)
    B["a1_rsum"] = evb_f[0:1, 0:TB]
    B["dres_a1"] = B["ln_xc"]
    B["dout_a1"] = B["ln_sq"]
    qsb = xln[:, 24 * TB:28 * TB]
    pbuf = xln[:, 28 * TB:32 * TB]
    gk = g_k1.rearrange("(r d) t -> d r t", d=D)
    gvv = g_v1

    def kview_fn(h):
        return gk[h * 128:(h + 1) * 128, :, 0:T]

    def vview_fn(h, r):
        return gvv.rearrange("(r w) d -> w r d", w=T)[:, r, h * 128:(h + 1) * 128] \
            .rearrange("(t8 p) d -> p t8 d", p=128)

    end_pe = emit_attn_core(ctx, B, "a1", 32, q1t, kview_fn, vview_fn, ao, qsb, pbuf,
                            arAb, arBb)

    wosb = B.sb("a1_wosb", [128, D], BF16)
    r_wo = ctx.rot("a1_wo", 1)
    for b in range(NB):
        for ot in range(NC):
            par = ot % 2
            r_wo.acquire(ctx, "gpsimd", 0)
            lw = ctx.dma("gpsimd", wosb[:], io["wo1_blk"][ot], "a1wo")
            ctx.waitm("tensor", lw)
            ctx.bacq("tensor", par)
            pm = None
            for hc in range(NC):
                slot = hc * NB + b
                pm = ctx.op("tensor", lambda e, par=par, hc=hc, slot=slot: e.matmul(
                    B.bank(par)[:, :], wosb[:, hc * 128:(hc + 1) * 128],
                    ao[:, slot * TB:(slot + 1) * TB], start=(hc == 0), stop=(hc == NC - 1)),
                    milestone=(hc == NC - 1))
            r_wo.release(0, pm)
            ctx.waitm("vector", pm)
            dv, sv = emit_delta_store(ctx, B, X_src, X_dst, b, B.bank(par)[:, :], ot,
                                      "a1", bias_col=boT[:, ot:ot + 1])
            ctx.brel(par, dv)
            if b == NB - 1 and ot == NC - 1:
                ctx.rot("xln_guard", 1).release(0, pm)



def build_program(sublayers, debug=False):
    nc = bass.Bass("TRN2", target_bir_lowering=False, debug=False)
    x_in = nc.dram_tensor("x_t", [D, T], F32, kind="ExternalInput").ap()
    y_out = nc.dram_tensor("y_t", [D, T], F32, kind="ExternalOutput").ap()
    xa = nc.dram_tensor("X_a", [D, T], F32).ap()
    xb = nc.dram_tensor("X_b", [D, T], F32).ap()
    dbg = None
    if debug:
        dbg = {
            "d_xln": nc.dram_tensor("d_xln", [128, NC * TB], BF16,
                                    kind="ExternalOutput").ap(),
            "d_w": nc.dram_tensor("d_w", [128, 32], F32, kind="ExternalOutput").ap(),
        }

    ins = {}

    def din(name, shape, dt):
        if name not in ins:
            ins[name] = nc.dram_tensor(name, shape, dt, kind="ExternalInput").ap()
        return ins[name]

    with ExitStack() as stack:
        B = Bufs(nc, stack)
        ctx = Ctx(nc, stack)
        block = stack.enter_context(nc.Block())

        loads = [
            ("ones_row_f", [1, TB], F32, "c_ones_f"),
            ("ones_row_b", [1, TB], BF16, "c_ones_b"),
            ("ones_col_f", [128, 1], F32, "c_ones_colf"),
            ("ones_col_b", [128, 1], BF16, "c_ones_colb"),
            ("identf", [128, 128], F32, "c_identf"),
            ("identb", [128, 128], BF16, "c_identb"),
            ("emat", [8, 128], F32, "c_emat"),
        ]
        for nm, shp, dt, src in loads:
            t = B.sb(nm, shp, dt)
            ctx.dma("sync", t[:], din(src, shp, dt), "cst_" + nm)
        for e in ENGINES:
            for nm, shp, dt, src in loads:
                ctx.wait(e, "D_cst_" + nm, 16)

        out_streams = []
        for i, sl_name in enumerate(sublayers):
            X_src = x_in if i == 0 else (xa if (i - 1) % 2 == 0 else xb)
            X_dst = y_out if i == len(sublayers) - 1 else (xa if i % 2 == 0 else xb)
            io = {"X_src": X_src, "X_dst": X_dst, "dbg": dbg}
            if sl_name == "ffn0":
                io.update({
                    "wbT_blk": din("wbT_blk", [NC, 128, NC * 128], F32),
                    "b_base": din("b_base", [1, D], F32),
                    "acat_blk": din("acat_blk", [128, NC, 128], F32),
                    "bcat": din("bcat", [128, D], F32),
                    "wrT": din("wrT", [D, 8], F32),
                    "b_router": din("b_router", [1, 8], F32),
                })
                emit_ffn0_setup(ctx, B, io)
                emit_ffn0(ctx, B, io)
                out_streams = ["f0os0", "f0os1"]
            elif sl_name == "ffn1":
                io.update({
                    "w1s": din("w1s", [11 * 128, 2048], I32),
                    "w3s": din("w3s", [11 * 128, 2048], I32),
                    "w2s": din("w2s", [D, 704], I32),
                    "s1s": din("s1s", [11 * 128, 64], F32),
                    "s3s": din("s3s", [11 * 128, 64], F32),
                    "s2s": din("s2s", [D, 22], F32),
                    "mcs": [nc.dram_tensor("m_c%d" % k, [1408, 4096], BF16).ap()
                            for k in range(3)],
                    "gms": [nc.dram_tensor("g_m%d" % k, [NCORES * 1408, 4096], BF16)
                            for k in range(3)],
                })
                if "dq_done" not in B:
                    emit_dequant(ctx, B, io)
                    B["dq_done"] = True
                emit_ffn1(ctx, B, io)
                out_streams = ["f1os0", "f1os1"]
            elif sl_name == "attn1":
                io.update({
                    "Wq1": din("Wq1", [D, D], F32),
                    "Wk1": din("Wk1", [D, D], F32),
                    "Wv1": din("Wv1", [D, D], F32),
                    "wo1_blk": din("wo1_blk", [NC, 128, NC * 128], F32),
                    "bq1": din("bq1", [1, D], F32),
                    "bk1": din("bk1", [1, D], F32),
                    "bv1": din("bv1", [1, D], F32),
                    "bo1": din("bo1", [1, D], F32),
                    "gq1": din("gq1", [1, 128], F32),
                    "gk1": din("gk1", [1, 128], F32),
                    "q1t": nc.dram_tensor("q1t", [D, T], BF16).ap(),
                    "k1c": nc.dram_tensor("k1c", [D, T], BF16).ap(),
                    "v1c": nc.dram_tensor("v1c", [T, D], BF16).ap(),
                    "g_k1": nc.dram_tensor("g_k1", [NCORES * D, T], BF16),
                    "g_v1": nc.dram_tensor("g_v1", [NCORES * T, D], BF16),
                })
                emit_attn1(ctx, B, io)
                out_streams = ["a1os0", "a1os1"]
            elif sl_name == "attn0":
                io.update({
                    "wq0": din("wq0", [D, D], F32),
                    "wk0_blk": din("wk0_blk", [8, 128, NC * 128], F32),
                    "wv0": din("wv0", [D, 1024], F32),
                    "wo0_blk": din("wo0_blk", [NC, 128, 8 * 128], F32),
                    "q0t": nc.dram_tensor("q0t", [1024, T], BF16).ap(),
                    "kv0c": nc.dram_tensor("kv0c", [2 * T, 1024], BF16).ap(),
                    "kv0g": nc.dram_tensor("kv0g", [NCORES * 2 * T, 1024], BF16),
                })
                emit_attn0(ctx, B, io)
                out_streams = ["a0os0", "a0os1"]
            else:
                raise NotImplementedError(sl_name)

        for st_name in out_streams:
            nm = "D_" + st_name
            if nm in ctx.cnt:
                ctx.wait("gpsimd", nm, ctx.cnt[nm])

        @block.sync
        def _(eng):
            for fn in ctx.ops["sync"]:
                fn(eng)

        @block.tensor
        def _(eng):
            for fn in ctx.ops["tensor"]:
                fn(eng)

        @block.vector
        def _(eng):
            for fn in ctx.ops["vector"]:
                fn(eng)

        @block.scalar
        def _(eng):
            for fn in ctx.ops["scalar"]:
                fn(eng)

        @block.gpsimd
        def _(eng):
            for fn in ctx.ops["gpsimd"]:
                fn(eng)

    return nc


SEQ = ["ffn0", "attn0", "ffn0", "ffn1", "attn1", "ffn1"]


def kernel(**inputs):
    per_core = host_prep(inputs)
    progs = {}
    for name in dict.fromkeys(SEQ):
        progs[name] = build_program([name])
    cur = [pc["x_t"] for pc in per_core]
    for name in SEQ:
        in_maps = [{**per_core[i], "x_t": cur[i]} for i in range(NCORES)]
        res = run_bass_kernel_spmd(progs[name], in_maps,
                                   core_ids=list(range(NCORES))).results
        cur = [np.asarray(r["y_t"]) for r in res]
    out = np.concatenate([c.T for c in cur], axis=0)
    return out.reshape(1, S, D).astype(np.float32)
